# version 02 (final)
# baseline (speedup 1.0000x reference)
import contextlib
import numpy as np
import concourse.bass as bass
import concourse.mybir as mybir
from concourse.bass_core_rust import IndirectOffsetOnAxis
from concourse.bass_utils import run_bass_kernel_spmd

F32 = mybir.dt.float32
I32 = mybir.dt.int32
U8 = mybir.dt.uint8
AX = mybir.AxisListType.X
ALU = mybir.AluOpType
ACT = mybir.ActivationFunctionType

T = 2048
NT = 16
D = 1024
HID_A = 512
NH_A, DH_A = 8, 64
SCALE_A = DH_A ** -0.5
NH_B, DH_B = 16, 64
SCALE_B = DH_B ** -0.5
WM = 512
S_FULL = 4096
ASSOC = S_FULL - WM
NE = 8
DFF = 4096
CAP = 2560
CL = 384
EPS = 1e-5
PAIRS = [[0, 1], [2, 3], [4, 5], [6, 7]]
ALL8 = [list(range(8))]

PHASES = 4
import os
STOP = int(os.environ.get("KSTOP", "99"))
KSUB = int(os.environ.get("KSUB", "99"))

ENGINES = ("sync", "gpsimd", "vector", "scalar", "tensor")


class Prog:

    def __init__(self, nc, es):
        self.nc = nc
        self.streams = {e: [] for e in ENGINES}
        self.es = es
        self.cnt = {e: 0 for e in ENGINES}
        self.dcnt = {(e, None): 0 for e in ENGINES}
        self.sems = {e: es.enter_context(nc.semaphore("sem_" + e)) for e in ENGINES}
        self.dsems = {(e, None): es.enter_context(nc.semaphore("dsem_" + e)) for e in ENGINES}
        self.events = {}

    def do(self, eng, fn, ev=None, dma=False, dsem=None):
        if dma:
            key = (eng, dsem)
            if key not in self.dcnt:
                self.dcnt[key] = 0
                self.dsems[key] = self.es.enter_context(
                    self.nc.semaphore(f"dsem_{eng}_{dsem}"))
            self.dcnt[key] += 16
            if ev is not None:
                assert ev not in self.events, ev
                self.events[ev] = ("d", key, self.dcnt[key])
        elif ev is not None:
            self.cnt[eng] += 1
            assert ev not in self.events, ev
            self.events[ev] = ("c", eng, self.cnt[eng])
        self.streams[eng].append(("do", fn, ev, dma, dsem))

    def wait(self, eng, ev):
        assert isinstance(ev, str)
        self.streams[eng].append(("wait", ev))

    def realize(self, block):
        def make(eng_name):
            def body(h):
                for item in self.streams[eng_name]:
                    if item[0] == "wait":
                        kind, e, v = self.events[item[1]]
                        h.wait_ge(self.sems[e] if kind == "c" else self.dsems[e], v)
                    else:
                        _, fn, ev, dma, dsem = item
                        inst = fn(h)
                        if dma:
                            inst.then_inc(self.dsems[(eng_name, dsem)], 16)
                        elif eng_name in ("vector", "scalar"):
                            d = h.drain()
                            if ev is not None:
                                d.then_inc(self.sems[eng_name], 1)
                        elif ev is not None:
                            if inst is None:
                                inst = h.nop()
                            inst.then_inc(self.sems[eng_name], 1)
            return body

        block.sync(make("sync"))
        block.gpsimd(make("gpsimd"))
        block.vector(make("vector"))
        block.scalar(make("scalar"))
        block.tensor(make("tensor"))


def build_nc(phases=PHASES):
    nc = bass.Bass("TRN2", target_bir_lowering=False, detect_race_conditions=False)
    es = contextlib.ExitStack()

    def ein(name, shape, dtype=F32):
        return nc.dram_tensor(name, shape, dtype, kind="ExternalInput").ap()

    x_in = ein("x", [T, D])
    aWq, aWk, aWv = ein("aWq", [D, HID_A]), ein("aWk", [D, HID_A]), ein("aWv", [D, HID_A])
    aWo, abo = ein("aWo", [HID_A, D]), ein("abo", [1, D])
    if phases >= 2:
        fWr = ein("fWr", [D, NE])
        fW1 = ein("fW1", [NE, D, DFF])
        fW2 = ein("fW2", [NE, DFF, D])
        maskprev = ein("maskprev", [8, 1])
        onehotc = ein("onehotc", [8, 1])
    if phases >= 3:
        bwq, bwk, bwv, bwo = (ein("bwq", [D, D]), ein("bwk", [D, D]),
                              ein("bwv", [D, D]), ein("bwo", [D, D]))
        bbq, bbk, bbv, bbo = (ein("bbq", [1, D]), ein("bbk", [1, D]),
                              ein("bbv", [1, D]), ein("bbo", [1, D]))
        bwqi, bwki = ein("bwqi", [D, 64]), ein("bwki", [D, 64])
        bbqi, bbki = ein("bbqi", [1, 64]), ein("bbki", [1, 64])
        halfsel = ein("halfsel", [1, 1], I32)
    y_out = nc.dram_tensor("y", [T, D], F32, kind="ExternalOutput").ap()
    KDBG = int(os.environ.get("KDBG", "0"))
    KDBG2 = int(os.environ.get("KDBG2", "0"))
    if KDBG2 and phases >= 3:
        d_shk = nc.dram_tensor("d_shk", [2, 128, 256], F32, kind="ExternalOutput").ap()
        d_qfm = nc.dram_tensor("d_qfm", [128, 256], F32, kind="ExternalOutput").ap()
        d_ofm = nc.dram_tensor("d_ofm", [2, 65, 256], F32, kind="ExternalOutput").ap()
        d_thr = nc.dram_tensor("d_thr", [16, 128], F32, kind="ExternalOutput").ap()
        d_shv = nc.dram_tensor("d_shv", [2, 128, 64], F32, kind="ExternalOutput").ap()
    if KDBG:
        dbg1 = nc.dram_tensor("dbg1", [128, 80], F32, kind="ExternalOutput").ap()
        dbg2 = nc.dram_tensor("dbg2", [128, NT * HID_A], F32, kind="ExternalOutput").ap()
        dbg3 = nc.dram_tensor("dbg3", [128, 512], F32, kind="ExternalOutput").ap()
        dbg4 = nc.dram_tensor("dbg4", [128, 512], F32, kind="ExternalOutput").ap()

    CCA = 512 + 64 * 512
    cc_a_in = nc.dram_tensor("cc_a_in", [CCA], F32, kind="Internal").ap()
    cc_a_out = nc.dram_tensor("cc_a_out", [CCA], F32, kind="Internal").ap()
    if phases >= 2:
        hist_in = {}
        hist_out = {}
        for ff in (0, 1):
            hist_in[ff] = nc.dram_tensor(f"hist_in{ff}", [8, 8], F32, kind="Internal").ap()
            hist_out[ff] = nc.dram_tensor(f"hist_out{ff}", [8, 8], F32, kind="Internal").ap()
        xln_dram = nc.dram_tensor("xln_dram", [T + 4, D], F32, kind="Internal").ap()
        eout_dram = nc.dram_tensor("eout_dram", [T + 4, D], F32, kind="Internal").ap()
        tokof_dram = nc.dram_tensor("tokof_dram", [NE * CL + 128], F32, kind="Internal").ap()
    if phases >= 3:
        sh_k = nc.dram_tensor("sh_k", [2, 8, 128, T], F32, kind="Internal",
                              addr_space="Shared").ap()
        sh_v = nc.dram_tensor("sh_v", [2, T, D], F32, kind="Internal",
                              addr_space="Shared").ap()
        sh_ki = nc.dram_tensor("sh_ki", [2, 64, T], F32, kind="Internal",
                               addr_space="Shared").ap()
        k_stage = nc.dram_tensor("k_stage", [8, 128, T], F32, kind="Internal").ap()
        v_stage = nc.dram_tensor("v_stage", [T, D], F32, kind="Internal").ap()
        ki_stage = nc.dram_tensor("ki_stage", [64, T], F32, kind="Internal").ap()
        qfm_dram = nc.dram_tensor("qfm_dram", [8, 128, T], F32, kind="Internal").ap()
        ofm_dram = nc.dram_tensor("ofm_dram", [16, 65, T], F32, kind="Internal").ap()
        xpark_dram = nc.dram_tensor("xpark_dram", [T, D], F32, kind="Internal").ap()
        bar_in = nc.dram_tensor("bar_in", [1, 8], F32, kind="Internal").ap()
        bar_out = nc.dram_tensor("bar_out", [1, 8], F32, kind="Internal").ap()

    ident_h = nc.inline_tensor(np.eye(128, dtype=np.float32), name="c_ident")
    ones_h = nc.inline_tensor(np.ones((128, 1), dtype=np.float32), name="c_ones")
    onesr_h = nc.inline_tensor(np.ones((1, 512), dtype=np.float32), name="c_onesrow")
    if phases >= 2:
        u128_h = nc.inline_tensor(np.triu(np.ones((128, 128), np.float32), 1), name="c_u128")
        u16_h = nc.inline_tensor(np.triu(np.ones((16, 16), np.float32), 1), name="c_u16")
        iota8_h = nc.inline_tensor(
            np.broadcast_to(np.arange(8, dtype=np.float32), (128, 8)).copy(), name="c_iota8")
        tokid_np = (np.arange(NT)[None, :] * 128 + np.arange(128)[:, None]).astype(np.float32)
        tokid_h = nc.inline_tensor(tokid_np, name="c_tokid")
        trash_h = nc.inline_tensor(np.full((NE * CL + 128,), float(T), dtype=np.float32), name="c_trash")
        zeros4_h = nc.inline_tensor(np.zeros((4, D), dtype=np.float32), name="c_zeros4")

    def sbuf(name, shape, dtype=F32):
        return es.enter_context(nc.sbuf_tensor(name, shape, dtype))

    xs = sbuf("xs", [128, NT, D])
    ident = sbuf("ident", [128, 128])
    ones_c = sbuf("ones_c", [128, 1])
    ones_r = sbuf("ones_r", [1, 512])
    wq = sbuf("wq", [128, 8, HID_A])
    wk = sbuf("wk", [128, 8, HID_A])
    wv = sbuf("wv", [128, 8, HID_A])
    wo = sbuf("wo", [128, 4, D])
    sq_scr = sbuf("sq_scr", [128, D])
    ssq = sbuf("ssq", [128, NT])
    mean = sbuf("mean", [128, NT])
    rstd = sbuf("rstd", [128, NT])
    lnt = sbuf("lnt", [128, 2, D])
    xfm = sbuf("xfm", [128, 2, 8, 128])
    ekt = sbuf("ekt", [128, 2, HID_A])
    vt = sbuf("vt", [128, 2, HID_A])
    q_sm = sbuf("q_sm", [128, NT, HID_A])
    qmax8 = sbuf("qmax8", [128, 8])
    qsum8 = sbuf("qsum8", [128, 8])
    lsum_s = sbuf("lsum_s", [128, 4])
    gsum_s = sbuf("gsum_s", [128, 4])
    uA = sbuf("uA", [128, 3072])
    aot = uA[:, 0:512]
    aofm = uA[:, 512:1024].rearrange("p (a b) -> p a b", a=4)
    ctx2 = uA[:, 1024:1536]
    lctx_s = uA[0:64, 1536:2048]
    qfm8 = uA[0:64, 2048:3072].rearrange("p (a b) -> p a b", a=8)
    qv_dbg = sbuf("qv_dbg", [128, 32])
    if phases >= 2:
        h1_fm = None
        u16c = sbuf("u16c", [16, 16])
        iota8c = sbuf("iota8c", [128, 8])
        u128c = sbuf("u128c", [128, 128])
        tokidc = sbuf("tokidc", [128, NT])
        Mbuf = sbuf("Mbuf", [128, 128])
        wb = sbuf("wb", [128, 128])
        tp_rep = sbuf("tp_rep", [128, 128])
        nb_rep = sbuf("nb_rep", [128, 8])
        hall = sbuf("hall", [8, 8])
        hrow = sbuf("hrow", [1, 8])
        hrep8 = sbuf("hrep8", [8, 8])
        mcontrib = sbuf("mcontrib", [8, 8])
        onec_s = sbuf("onec_s", [8, 1])
        mprev_s = sbuf("mprev_s", [8, 1])
        t16 = sbuf("t16", [16, 8])
        tpf = sbuf("tpf", [16, 8])
        totrow = sbuf("totrow", [1, 128])
        rlog = sbuf("rlog", [128, 8])
        t8b = sbuf("t8b", [128, 8])
        midx = sbuf("midx", [128, 8], mybir.dt.uint32)
        idxf = sbuf("idxf", [128, 1])
        lmaxn = sbuf("lmaxn", [128, 1])
        escr = sbuf("escr", [128, 8])
        pmaxb = sbuf("pmaxb", [128, NT])
        s1b = sbuf("s1b", [128, NT])
        s0b = sbuf("s0b", [128, NT])
        slot_i = sbuf("slot_i", [128, NT], I32)
        vsc1 = sbuf("vsc1", [128, 1])
        vsc2 = sbuf("vsc2", [128, 1])
        vsc3 = sbuf("vsc3", [128, 1])
        tok_sb = sbuf("tok_sb", [128, 25], I32)
        tk25 = sbuf("tk25", [25, 128])
        tokf = sbuf("tokf", [128, 25])
        wrbuf = sbuf("wrbuf", [128, 8, 8])
    if phases >= 3:
        ofm_s = uA[0:65, 0:2048]
        thrb = sbuf("thrb", [128, 16])
        rden = sbuf("rden", [128, 16])
        bqfm = sbuf("bqfm", [128, 8])
        bkfm = sbuf("bkfm", [128, 8])
        qibias = sbuf("qibias", [64, 1])
        kibias = sbuf("kibias", [64, 1])
        bstg = tk25[0:8, :]
        halfsb = sbuf("halfsb", [1, 1], I32)
        hreg = es.enter_context(nc.gpsimd.register("hreg"))
        xs_flat = xs[:].rearrange("p t d -> p (t d)")
        xfull = xs_flat[:, 0:16384].rearrange("p (c n) -> p c n", c=8)
        mask_u8 = xs_flat[:, 0:14336].bitcast(U8).rearrange("p (a n) -> p a n", a=28)
        thr_rep = xs_flat[:, 14336:16384]
        kbuf = q_sm[0:64, :, :].rearrange("p t d -> p (t d)").rearrange(
            "p (u n) -> p u n", u=2)
        wo_s64 = q_sm[0:64, :, :].rearrange("p t d -> p (t d)").rearrange(
            "p (h n) -> p h n", h=16)
        vws = q_sm[:, :, :].rearrange("p t d -> p (t d)").rearrange(
            "p (c n) -> p c n", c=8)
        Ibuf = wq[:].rearrange("p a b -> p (a b)")[:, 0:3584]
        ostage = wq[:].rearrange("p a b -> p (a b)")[0:64, :].rearrange(
            "p (u h n) -> p u h n", u=2, h=16)
        denacc = wq[:].rearrange("p a b -> p (a b)")[64:65, 0:2048]
        qbuf = wk[0:64, :, :].rearrange("p a b -> p (a b)").rearrange(
            "p (u n) -> p u n", u=2)
        kidx_all = wo[0:64, :, :].rearrange("p a b -> p (a b)")
        qidx_fm = xfm[0:64, :, :, :].rearrange("p a b c -> p (a b c)")
        bv_rep = wv[:].rearrange("p a b -> p (a b)")[:, 0:1024]
        vstg = wv[:].rearrange("p a b -> p (a b)")[:, 1024:2048]
        vstage = wv[:].rearrange("p a b -> p (a b)")[:, 0:2080].rearrange(
            "p (a n) -> p a n", a=32)
        Wbuf = lnt[:].rearrange("p a b -> p (a b)")
        wqstr = lnt[:].rearrange("p a b -> p (a b)").rearrange(
            "p (u c n) -> p u c n", u=2, c=8)
        bo_rep2 = sq_scr[:]
        w1buf = lnt[:].rearrange("p a b -> p (a b)").rearrange(
            "p (u c n) -> p u c n", u=2, c=8)
        w2buf = xfm[:].rearrange("p a b c -> p a (b c)")
        eoutt = sq_scr[:]
        bl_a = ekt[:].rearrange("p a b -> p (a b)")
        bl_b = vt[:].rearrange("p a b -> p (a b)")
    bo_rep = vt[:].rearrange("p a b -> p (a b)")
    rsc_rep = ekt[:, 0, :]

    def psum(name, shape):
        return es.enter_context(nc.psum_tensor(name, shape, F32))

    PS = [psum("ps%d" % i, [128, 512]) for i in range(8)]

    P = Prog(nc, es)

    P.do("sync", lambda e: e.dma_start(out=xs[:], in_=x_in.rearrange("(t p) d -> p t d", p=128)), dma=True)
    P.do("sync", lambda e: e.dma_start(out=wq[:], in_=aWq.rearrange("(c p) n -> p c n", p=128)), dma=True)
    P.do("sync", lambda e: e.dma_start(out=wk[:], in_=aWk.rearrange("(c p) n -> p c n", p=128)), dma=True)
    P.do("sync", lambda e: e.dma_start(out=wv[:], in_=aWv.rearrange("(c p) n -> p c n", p=128)), dma=True)
    P.do("sync", lambda e: e.dma_start(out=wo[:], in_=aWo.rearrange("(c p) n -> p c n", p=128)), dma=True)

    P.do("sync", lambda e: e.dma_start(out=ident[:], in_=ident_h.ap()[:, :]), dma=True)
    P.do("sync", lambda e: e.dma_start(out=ones_c[:], in_=ones_h.ap()[:, :]), dma=True)
    P.do("sync", lambda e: e.dma_start(out=ones_r[:], in_=onesr_h.ap()[:, :]), ev="loads_a", dma=True)

    if phases >= 2:
        dtm = q_sm[:, 0:6, :].rearrange("p a b -> p (a b)").rearrange(
            "p (r d) -> p r d", r=3)
        dfm = q_sm[:, 6:12, :].rearrange("p a b -> p (a b)").rearrange(
            "p (c r) -> p c r", c=8)

        def h1v(f):
            base, off = ((wq, f) if f < 10 else (wk, f - 10) if f < 20 else
                         (wv, f - 20) if f < 30 else (wo, f - 30))
            fl = base[:].rearrange("p a b -> p (a b)")
            return fl[:, off * 384:off * 384 + 384]

        P.do("sync", lambda e: e.dma_start(out=u128c[:], in_=u128_h.ap()[:, :]), dma=True)
        P.do("sync", lambda e: e.dma_start(out=u16c[:], in_=u16_h.ap()[:, :]), dma=True)
        P.do("sync", lambda e: e.dma_start(out=iota8c[:], in_=iota8_h.ap()[:, :]), dma=True)
        P.do("sync", lambda e: e.dma_start(out=tokidc[:], in_=tokid_h.ap()[:, :]), dma=True)
        P.do("sync", lambda e: e.dma_start(out=onec_s[:], in_=onehotc[:, :]), dma=True)
        P.do("sync", lambda e: e.dma_start(out=mprev_s[:], in_=maskprev[:, :]), dma=True)
        P.do("sync", lambda e: e.dma_start(out=wrbuf[:], in_=fWr.rearrange("(c p) n -> p c n", p=128)), dma=True)
        P.do("sync", lambda e: e.dma_start(out=xln_dram[T:T + 4, :], in_=zeros4_h.ap()[:, :]),
             ev="f_consts", dma=True)

    def emit_ln_stats(tag):
        if tag == "a":
            P.wait("vector", "loads_a")
        for t in range(NT):
            P.do("vector", lambda e, t=t: e.reduce_sum(mean[:, t:t + 1], xs[:, t, :], axis=AX))
            P.do("vector", lambda e, t=t: e.tensor_tensor(
                sq_scr[:], xs[:, t, :], xs[:, t, :], ALU.mult))
            P.do("vector", lambda e, t=t: e.reduce_sum(ssq[:, t:t + 1], sq_scr[:], axis=AX))
        P.do("vector", lambda e: e.tensor_scalar_mul(mean[:], mean[:], 1.0 / D))
        P.do("vector", lambda e: e.tensor_scalar_mul(rstd[:], ssq[:], 1.0 / D))
        P.do("vector", lambda e: e.tensor_tensor(sq_scr[:, :NT], mean[:], mean[:], ALU.mult))
        P.do("vector", lambda e: e.tensor_tensor(rstd[:], rstd[:], sq_scr[:, :NT], ALU.subtract))
        P.do("vector", lambda e: e.tensor_scalar_add(rstd[:], rstd[:], EPS))
        if tag == "a":
            P.do("vector", lambda e: e.tensor_copy(qv_dbg[:, 0:16], rstd[:]))
        P.do("vector", lambda e: e.reciprocal(rstd[:], rstd[:]), ev=f"{tag}_var")
        if tag == "a":
            P.wait("vector", f"{tag}_var")
            P.do("vector", lambda e: e.tensor_copy(qv_dbg[:, 16:32], rstd[:]))
        P.wait("scalar", f"{tag}_var")
        P.do("scalar", lambda e: e.activation(out=rstd[:], in_=rstd[:], func=ACT.Sqrt),
             ev=f"{tag}_rstd")

    if STOP >= 1:
        emit_ln_stats("a")
    else:
        P.events["a_rstd"] = P.events["loads_a"]
    P.events.setdefault("a_phase_end", P.events["a_rstd"])

    last = ["a_rstd"]
    for t in range(NT if STOP >= 2 else 0):
        m = t % 2
        P.wait("vector", "a_rstd") if t == 0 else None
        if t >= 2:
            P.wait("vector", f"a_tp_{t-2}_7")
        P.do("vector", lambda e, t=t, m=m: e.tensor_scalar(
            lnt[:, m, :], xs[:, t, :], mean[:, t:t + 1], rstd[:, t:t + 1],
            ALU.subtract, op1=ALU.mult), ev=f"a_ln_{t}")

        P.wait("tensor", f"a_ln_{t}")
        for c in range(8):
            k = t * 8 + c
            if k >= 2:
                P.wait("tensor", f"a_xf_{(k - 2) // 8}_{(k - 2) % 8}")
            P.do("tensor", lambda e, m=m, c=c, k=k: e.transpose(
                PS[k % 2][:, 0:128], lnt[:, m, c * 128:(c + 1) * 128], ident[:]),
                ev=f"a_tp_{t}_{c}")
            P.wait("scalar", f"a_tp_{t}_{c}")
            P.do("scalar", lambda e, m=m, c=c, k=k: e.copy(
                out=xfm[:, m, c, :], in_=PS[k % 2][:, 0:128]), ev=f"a_xf_{t}_{c}")

        last[0] = f"a_xf_{t}_7"
        if STOP < 3:
            continue
        P.wait("tensor", f"a_xf_{t}_7")
        if t >= 1 and STOP >= 4:
            P.wait("tensor", f"a_qsub_{t-1}")
        for c in range(8):
            P.do("tensor", lambda e, m=m, c=c: e.matmul(
                PS[3][:], xfm[:, m, c, :], wq[:, c, :], start=(c == 0), stop=(c == 7)),
                ev=f"a_qps_{t}" if c == 7 else None)
        if t >= 1 and STOP >= 4:
            P.wait("tensor", f"a_v_{t-1}")
        for c in range(8):
            P.do("tensor", lambda e, m=m, c=c: e.matmul(
                PS[2][:], xfm[:, m, c, :], wk[:, c, :], start=(c == 0), stop=(c == 7)),
                ev=f"a_kps_{t}" if c == 7 else None)

        last[0] = f"a_kps_{t}"
        if STOP < 4:
            continue
        P.wait("scalar", f"a_kps_{t}")
        if t >= 2:
            P.wait("scalar", f"a_ctx_{t-2}")
        P.do("scalar", lambda e, m=m: e.activation(
            out=ekt[:, m, :], in_=PS[2][:], func=ACT.Exp), ev=f"a_ek_{t}")

        P.wait("tensor", f"a_ek_{t}")
        for c in range(8):
            P.do("tensor", lambda e, m=m, c=c: e.matmul(
                PS[2][:], xfm[:, m, c, :], wv[:, c, :], start=(c == 0), stop=(c == 7)),
                ev=f"a_vps_{t}" if c == 7 else None)
        P.wait("scalar", f"a_vps_{t}")
        P.do("scalar", lambda e, m=m: e.copy(out=vt[:, m, :], in_=PS[2][:]), ev=f"a_v_{t}")

        P.wait("vector", f"a_qps_{t}")
        P.do("vector", lambda e: e.reduce_max(
            qmax8[:], PS[3][:].rearrange("p (s d) -> p s d", s=8), axis=AX))
        P.do("vector", lambda e, t=t: e.tensor_tensor(
            q_sm[:, t, :].rearrange("p (s d) -> p s d", s=8),
            PS[3][:].rearrange("p (s d) -> p s d", s=8),
            qmax8[:].unsqueeze(2).broadcast_to([128, 8, 64]),
            ALU.subtract), ev=f"a_qsub_{t}")
        P.wait("scalar", f"a_qsub_{t}")
        P.do("scalar", lambda e, t=t: e.activation(
            out=q_sm[:, t, :], in_=q_sm[:, t, :], func=ACT.Exp), ev=f"a_qexp_{t}")
        P.wait("vector", f"a_qexp_{t}")
        P.do("vector", lambda e, t=t: e.reduce_sum(
            qsum8[:], q_sm[:, t, :].rearrange("p (s d) -> p s d", s=8), axis=AX))
        P.do("vector", lambda e: e.reciprocal(qsum8[:], qsum8[:]))
        P.do("vector", lambda e: e.tensor_scalar_mul(qsum8[:], qsum8[:], SCALE_A))
        P.do("vector", lambda e, t=t: e.tensor_tensor(
            q_sm[:, t, :].rearrange("p (s d) -> p s d", s=8),
            q_sm[:, t, :].rearrange("p (s d) -> p s d", s=8),
            qsum8[:].unsqueeze(2).broadcast_to([128, 8, 64]),
            ALU.mult), ev=f"a_qsm_{t}")

        P.wait("tensor", f"a_v_{t}")
        for j in range(4):
            P.do("tensor", lambda e, m=m, j=j, t=t: e.matmul(
                PS[4][:, j:j + 1], ekt[:, m, j * 128:(j + 1) * 128], ones_c[:],
                start=(t == 0 and j == 0), stop=(t == NT - 1 and j == 3),
                skip_group_check=True))
        for hh in range(NH_A):
            P.do("tensor", lambda e, m=m, hh=hh, t=t: e.matmul(
                PS[5][0:64, hh * 64:(hh + 1) * 64], ekt[:, m, hh * 64:(hh + 1) * 64],
                vt[:, m, hh * 64:(hh + 1) * 64],
                start=(t == 0 and hh == 0), stop=(t == NT - 1 and hh == NH_A - 1),
                skip_group_check=True),
                ev=f"a_ctx_{t}" if hh == NH_A - 1 else None)

    if STOP >= 4:
        last[0] = f"a_ctx_{NT-1}"
    if STOP < 5:
        P.events["a_phase_end"] = P.events[last[0]]
    else:
        P.wait("scalar", f"a_ctx_{NT-1}")
        P.do("scalar", lambda e: e.copy(out=lsum_s[:], in_=PS[4][:, 0:4]))
        P.do("scalar", lambda e: e.copy(out=lctx_s, in_=PS[5][0:64, :]), ev="a_stats_ev")
        P.wait("gpsimd", "a_stats_ev")
        P.do("gpsimd", lambda e: e.dma_start(
            out=cc_a_in[0:512].rearrange("(p c) -> p c", p=128), in_=lsum_s[:]), dma=True)
        P.do("gpsimd", lambda e: e.dma_start(
            out=cc_a_in[512:CCA].rearrange("(p c) -> p c", p=64), in_=lctx_s),
            ev="a_ccin", dma=True)
        P.wait("gpsimd", "a_ccin")
        P.do("gpsimd", lambda e: e.collective_compute(
            "AllReduce", ALU.add, replica_groups=PAIRS,
            ins=[cc_a_in[:].opt()], outs=[cc_a_out[:].opt()]), ev="a_cc")
        P.wait("gpsimd", "a_cc")
        P.do("gpsimd", lambda e: e.dma_start(
            out=gsum_s[:], in_=cc_a_out[0:512].rearrange("(p c) -> p c", p=128)), dma=True)
        P.do("gpsimd", lambda e: e.dma_start(
            out=ctx2[0:64, :], in_=cc_a_out[512:CCA].rearrange("(p c) -> p c", p=64)),
            ev="a_gctx", dma=True)
        P.wait("sync", f"a_ctx_{NT-1}")
        P.do("sync", lambda e: e.dma_start(out=bo_rep, in_=abo[:, :].broadcast_to([128, D])),
             ev="a_borep", dma=True)
        P.wait("tensor", "a_gctx")
        P.do("tensor", lambda e: e.transpose(PS[0][0:4, 0:128], gsum_s[:, :], ident[:]),
             ev="a_gtp")
        P.wait("scalar", "a_gtp")
        P.do("scalar", lambda e: e.copy(out=lctx_s[0:4, 0:128], in_=PS[0][0:4, 0:128]),
             ev="a_gtev")
        P.wait("gpsimd", "a_gtev")
        P.do("gpsimd", lambda e: e.dma_start(
            out=cc_a_in[0:512].rearrange("(j p) -> j p", j=4), in_=lctx_s[0:4, 0:128]),
            ev="a_gfl", dma=True)
        P.wait("gpsimd", "a_gfl")
        P.do("gpsimd", lambda e: e.dma_start(
            out=rsc_rep, in_=cc_a_in[0:512].unsqueeze(0).broadcast_to([128, 512])),
            ev="a_grep", dma=True)
        P.wait("vector", "a_grep")
        P.do("vector", lambda e: e.reciprocal(rsc_rep, rsc_rep), ev="a_rg")

        P.events["a_phase_end"] = P.events["a_rg"]
    for t in range(NT if STOP >= 6 else 0):
        m = t % 2
        P.wait("vector", "a_rg") if t == 0 else None
        P.wait("vector", f"a_qsm_{t}")
        P.do("vector", lambda e, t=t: e.tensor_tensor(
            q_sm[:, t, :], q_sm[:, t, :], rsc_rep, ALU.mult), ev=f"a_qg_{t}")
        P.wait("tensor", f"a_qg_{t}")
        for hh in range(NH_A):
            k = t * 8 + hh
            if k >= 2:
                P.wait("tensor", f"a_qf_{(k - 2) // 8}_{(k - 2) % 8}")
            P.do("tensor", lambda e, t=t, hh=hh, k=k: e.transpose(
                PS[k % 2][0:64, 0:128], q_sm[:, t, hh * 64:(hh + 1) * 64], ident[:]),
                ev=f"a_qtp_{t}_{hh}")
            P.wait("scalar", f"a_qtp_{t}_{hh}")
            P.do("scalar", lambda e, hh=hh, k=k: e.copy(
                out=qfm8[:, hh, :], in_=PS[k % 2][0:64, 0:128]), ev=f"a_qf_{t}_{hh}")
        if KSUB < 2:
            continue
        P.wait("tensor", f"a_qf_{t}_7")
        if t >= 1:
            P.wait("tensor", f"a_ao_{t-1}")
        for hh in range(NH_A):
            P.do("tensor", lambda e, hh=hh: e.matmul(
                PS[3][:, hh * 64:(hh + 1) * 64], qfm8[:, hh, :],
                ctx2[0:64, hh * 64:(hh + 1) * 64],
                start=(hh == 0), stop=(hh == NH_A - 1), skip_group_check=True),
                ev=f"a_aops_{t}" if hh == NH_A - 1 else None)
        P.wait("scalar", f"a_aops_{t}")
        P.do("scalar", lambda e: e.copy(out=aot, in_=PS[3][:]), ev=f"a_ao_{t}")
        if KSUB < 3:
            continue
        P.wait("tensor", f"a_ao_{t}")
        for j in range(4):
            k = t * 4 + j
            if j >= 2:
                P.wait("tensor", f"a_af_{t}_{j-2}")
            P.do("tensor", lambda e, j=j, k=k: e.transpose(
                PS[k % 2][:, 0:128], aot[:, j * 128:(j + 1) * 128], ident[:]),
                ev=f"a_atp_{t}_{j}")
            P.wait("scalar", f"a_atp_{t}_{j}")
            P.do("scalar", lambda e, j=j, k=k: e.copy(
                out=aofm[:, j, :], in_=PS[k % 2][:, 0:128]), ev=f"a_af_{t}_{j}")
        if KSUB < 4:
            continue
        P.wait("tensor", f"a_af_{t}_3")
        if t >= 1 and KSUB >= 5:
            P.wait("tensor", f"a_xadd_{t-1}")
        for j in range(4):
            for n in range(2):
                P.do("tensor", lambda e, j=j, n=n: e.matmul(
                    PS[6 + n][:], aofm[:, j, :],
                    wo[:, j, n * 512:(n + 1) * 512], start=(j == 0), stop=(j == 3)),
                    ev=f"a_yps_{t}" if (j == 3 and n == 1) else None)
        if KSUB < 5:
            continue
        if t == 0:
            P.wait("vector", "a_borep")
        P.wait("vector", f"a_yps_{t}")
        P.do("vector", lambda e, t=t: e.tensor_tensor(
            aot, PS[6][:], bo_rep[:, 0:512], ALU.add))
        P.do("vector", lambda e, t=t: e.tensor_tensor(
            xs[:, t, 0:512], xs[:, t, 0:512], aot, ALU.add))
        P.do("vector", lambda e, t=t: e.tensor_tensor(
            aot, PS[7][:], bo_rep[:, 512:1024], ALU.add))
        P.do("vector", lambda e, t=t: e.tensor_tensor(
            xs[:, t, 512:1024], xs[:, t, 512:1024], aot, ALU.add),
            ev=f"a_xadd_{t}")

    if STOP >= 6:
        endmap = {1: f"a_qf_{NT-1}_7", 2: f"a_ao_{NT-1}", 3: f"a_af_{NT-1}_3",
                  4: f"a_yps_{NT-1}", 5: f"a_xadd_{NT-1}"}
        P.events["a_phase_end"] = P.events[endmap[min(KSUB, 5)]]
    last_ev = "a_phase_end"

    def emit_ffn(tag, after_ev):
        hin, hout = (hist_in[0], hist_out[0]) if tag == "f1" else (hist_in[1], hist_out[1])
        P.wait("vector", after_ev)
        P.wait("vector", "f_consts")
        emit_ln_stats(tag)
        for t in range(NT):
            m = t % 2
            P.wait("vector", f"{tag}_rstd") if t == 0 else None
            if t >= 2:
                P.wait("vector", f"{tag}_tp_{t-2}_7")
                P.wait("vector", f"{tag}_xd_{t-2}")
            P.do("vector", lambda e, t=t, m=m: e.tensor_scalar(
                lnt[:, m, :], xs[:, t, :], mean[:, t:t + 1], rstd[:, t:t + 1],
                ALU.subtract, op1=ALU.mult), ev=f"{tag}_ln_{t}")
            P.wait("sync", f"{tag}_ln_{t}")
            P.do("sync", lambda e, t=t, m=m: e.dma_start(
                out=xln_dram[t * 128:(t + 1) * 128, :], in_=lnt[:, m, :]),
                ev=f"{tag}_xd_{t}", dma=True, dsem=f"xd{m}")
            P.wait("tensor", f"{tag}_ln_{t}")
            for c in range(8):
                k = t * 8 + c
                if k >= 2:
                    P.wait("tensor", f"{tag}_xf_{(k - 2) // 8}_{(k - 2) % 8}")
                P.do("tensor", lambda e, m=m, c=c, k=k: e.transpose(
                    PS[k % 2][:, 0:128], lnt[:, m, c * 128:(c + 1) * 128], ident[:]),
                    ev=f"{tag}_tp_{t}_{c}")
                P.wait("scalar", f"{tag}_tp_{t}_{c}")
                P.do("scalar", lambda e, m=m, c=c, k=k: e.copy(
                    out=xfm[:, m, c, :], in_=PS[k % 2][:, 0:128]), ev=f"{tag}_xf_{t}_{c}")
            P.wait("tensor", f"{tag}_xf_{t}_7")
            if t >= 1:
                P.wait("tensor", f"{tag}_rl_{t-1}")
            for c in range(8):
                P.do("tensor", lambda e, m=m, c=c: e.matmul(
                    PS[2][:, 0:8], xfm[:, m, c, :], wrbuf[:, c, :],
                    start=(c == 0), stop=(c == 7)),
                    ev=f"{tag}_rps_{t}" if c == 7 else None)
            P.wait("vector", f"{tag}_rps_{t}")
            P.do("vector", lambda e: e.tensor_copy(rlog[:], PS[2][:, 0:8]), ev=f"{tag}_rl_{t}")
            P.do("vector", lambda e: e.max(out=t8b[:], in_=rlog[:]))
            P.do("vector", lambda e: e.max_index(out=midx[:], in_max=t8b[:], in_values=rlog[:]))
            P.do("vector", lambda e: e.tensor_copy(idxf[:], midx[:, 0:1]))
            P.do("vector", lambda e: e.tensor_scalar_mul(lmaxn[:], t8b[:, 0:1], -1.0),
                 ev=f"{tag}_rv_{t}")
            P.wait("scalar", f"{tag}_rv_{t}")
            P.do("scalar", lambda e: e.activation(out=escr[:], in_=rlog[:], func=ACT.Exp,
                                                  bias=lmaxn[:], scale=1.0),
                 ev=f"{tag}_re_{t}")
            P.wait("vector", f"{tag}_re_{t}")
            P.do("vector", lambda e, t=t: e.reduce_sum(pmaxb[:, t:t + 1], escr[:], axis=AX))
            P.do("vector", lambda e, t=t: e.reciprocal(pmaxb[:, t:t + 1], pmaxb[:, t:t + 1]))
            P.do("vector", lambda e, t=t: e.tensor_tensor(
                Mbuf[:, t * 8:(t + 1) * 8], idxf[:].broadcast_to([128, 8]), iota8c[:],
                ALU.is_equal), ev=f"{tag}_mb_{t}")
        P.wait("tensor", f"{tag}_mb_{NT-1}")
        P.do("tensor", lambda e: e.matmul(PS[3][:, 0:128], u128c[:], Mbuf[:],
                                          start=True, stop=True), ev=f"{tag}_wps")
        P.do("tensor", lambda e: e.matmul(PS[0][0:1, 0:128], ones_c[:], Mbuf[:],
                                          start=True, stop=True), ev=f"{tag}_tots")
        P.wait("scalar", f"{tag}_wps")
        P.do("scalar", lambda e: e.copy(out=wb[:], in_=PS[3][:, 0:128]), ev=f"{tag}_wb")
        P.wait("scalar", f"{tag}_tots")
        P.do("scalar", lambda e: e.copy(out=totrow[:], in_=PS[0][0:1, 0:128]),
             ev=f"{tag}_totr")
        P.wait("gpsimd", f"{tag}_totr")
        P.do("gpsimd", lambda e: e.dma_start(
            out=cc_a_in[0:128].unsqueeze(0), in_=totrow[:]), ev=f"{tag}_totd", dma=True)
        P.wait("gpsimd", f"{tag}_totd")
        P.do("gpsimd", lambda e: e.dma_start(
            out=t16[:], in_=cc_a_in[0:128].rearrange("(t e) -> t e", t=16)),
            ev=f"{tag}_t16", dma=True)
        P.wait("tensor", f"{tag}_t16")
        P.do("tensor", lambda e: e.matmul(PS[1][0:16, 0:8], u16c[:], t16[:],
                                          start=True, stop=True), ev=f"{tag}_tpps")
        P.do("tensor", lambda e: e.matmul(PS[2][0:1, 0:8], ones_c[0:16, :], t16[:],
                                          start=True, stop=True), ev=f"{tag}_hps")
        P.wait("scalar", f"{tag}_tpps")
        P.do("scalar", lambda e: e.copy(out=tpf[:], in_=PS[1][0:16, 0:8]), ev=f"{tag}_tpf")
        P.wait("scalar", f"{tag}_hps")
        P.do("scalar", lambda e: e.copy(out=hrow[:], in_=PS[2][0:1, 0:8]), ev=f"{tag}_hrow")
        P.wait("gpsimd", f"{tag}_tpf")
        P.do("gpsimd", lambda e: e.dma_start(
            out=cc_a_in[0:128].rearrange("(t e) -> t e", t=16), in_=tpf[:]),
            ev=f"{tag}_tpd", dma=True)
        P.wait("gpsimd", f"{tag}_tpd")
        P.do("gpsimd", lambda e: e.dma_start(
            out=tp_rep[:], in_=cc_a_in[0:128].unsqueeze(0).broadcast_to([128, 128])),
            ev=f"{tag}_tprep", dma=True)
        P.wait("gpsimd", f"{tag}_hrow")
        P.do("gpsimd", lambda e: e.dma_start(
            out=cc_a_in[128:136].unsqueeze(0), in_=hrow[:]), ev=f"{tag}_hd", dma=True)
        P.wait("gpsimd", f"{tag}_hd")
        P.do("gpsimd", lambda e: e.dma_start(
            out=hrep8[:], in_=cc_a_in[128:136].unsqueeze(0).broadcast_to([8, 8])),
            ev=f"{tag}_hrep", dma=True)
        P.wait("vector", f"{tag}_hrep")
        P.do("vector", lambda e: e.tensor_tensor(
            mcontrib[:], hrep8[:], onec_s[:].broadcast_to([8, 8]), ALU.mult),
            ev=f"{tag}_mc")
        P.wait("gpsimd", f"{tag}_mc")
        P.do("gpsimd", lambda e: e.dma_start(out=hin[:, :], in_=mcontrib[:]),
             ev=f"{tag}_hin", dma=True)
        P.wait("gpsimd", f"{tag}_hin")
        P.do("gpsimd", lambda e, hin=hin, hout=hout: e.collective_compute(
            "AllReduce", ALU.add, replica_groups=ALL8,
            ins=[hin.opt()], outs=[hout.opt()]), ev=f"{tag}_cc")
        P.wait("gpsimd", f"{tag}_cc")
        P.do("gpsimd", lambda e, hout=hout: e.dma_start(out=hall[:], in_=hout[:, :]),
             ev=f"{tag}_hall", dma=True)
        P.wait("tensor", f"{tag}_hall")
        P.do("tensor", lambda e: e.matmul(PS[2][0:1, 8:16], mprev_s[:], hall[:],
                                          start=True, stop=True), ev=f"{tag}_nbps")
        P.wait("scalar", f"{tag}_nbps")
        P.do("scalar", lambda e: e.copy(out=hrow[:], in_=PS[2][0:1, 8:16]), ev=f"{tag}_nbr")
        P.wait("gpsimd", f"{tag}_nbr")
        P.do("gpsimd", lambda e: e.dma_start(
            out=cc_a_in[136:144].unsqueeze(0), in_=hrow[:]), ev=f"{tag}_nbd", dma=True)
        P.wait("gpsimd", f"{tag}_nbd")
        P.do("gpsimd", lambda e: e.dma_start(
            out=nb_rep[:], in_=cc_a_in[136:144].unsqueeze(0).broadcast_to([128, 8])),
            ev=f"{tag}_nbrep", dma=True)
        P.wait("vector", f"{tag}_tprep")
        P.wait("vector", f"{tag}_nbrep")
        P.wait("vector", f"{tag}_wb")
        for t in range(NT):
            sl = slice(t * 8, (t + 1) * 8)
            P.do("vector", lambda e, sl=sl: e.tensor_tensor(
                escr[:], wb[:, sl], tp_rep[:, sl], ALU.add))
            P.do("vector", lambda e, sl=sl: e.tensor_tensor(
                rlog[:], escr[:], Mbuf[:, sl], ALU.mult))
            P.do("vector", lambda e: e.reduce_sum(vsc1[:], rlog[:], axis=AX))
            P.do("vector", lambda e, sl=sl: e.tensor_tensor(
                rlog[:], nb_rep[:], Mbuf[:, sl], ALU.mult))
            P.do("vector", lambda e: e.reduce_sum(vsc2[:], rlog[:], axis=AX))
            P.do("vector", lambda e, sl=sl: e.tensor_tensor(
                rlog[:], iota8c[:], Mbuf[:, sl], ALU.mult))
            P.do("vector", lambda e: e.reduce_sum(vsc3[:], rlog[:], axis=AX))
            P.do("vector", lambda e: e.tensor_tensor(escr[:, 0:1], vsc1[:], vsc2[:], ALU.add))
            P.do("vector", lambda e: e.tensor_scalar(
                escr[:, 1:2], escr[:, 0:1], float(CAP), None, ALU.is_lt))
            P.do("vector", lambda e: e.tensor_scalar(
                escr[:, 2:3], vsc1[:], float(CL), None, ALU.is_lt))
            P.do("vector", lambda e, t=t: e.tensor_tensor(
                s1b[:, t:t + 1], escr[:, 1:2], escr[:, 2:3], ALU.mult))
            P.do("vector", lambda e, t=t: e.tensor_scalar(
                s0b[:, t:t + 1], s1b[:, t:t + 1], -1.0, 1.0, ALU.mult, op1=ALU.add))
            P.do("vector", lambda e, t=t: e.tensor_tensor(
                s1b[:, t:t + 1], s1b[:, t:t + 1], pmaxb[:, t:t + 1], ALU.mult))
            P.do("vector", lambda e: e.tensor_scalar(
                escr[:, 3:4], vsc3[:], float(CL), None, ALU.mult))
            P.do("vector", lambda e: e.tensor_tensor(
                escr[:, 3:4], escr[:, 3:4], vsc1[:], ALU.add))
            P.do("vector", lambda e: e.tensor_tensor(
                escr[:, 3:4], escr[:, 3:4], escr[:, 2:3], ALU.mult))
            P.do("vector", lambda e: e.tensor_scalar(
                escr[:, 4:5], escr[:, 2:3], -float(NE * CL), float(NE * CL),
                ALU.mult, op1=ALU.add))
            P.do("vector", lambda e: e.tensor_tensor(
                escr[:, 3:4], escr[:, 3:4], escr[:, 4:5], ALU.add))
            P.do("vector", lambda e, t=t: e.tensor_copy(slot_i[:, t:t + 1], escr[:, 3:4]),
                 ev=f"{tag}_sl_{t}")
        P.do("gpsimd", lambda e: e.dma_start(
            out=tokof_dram[:], in_=trash_h.ap()[:]), ev=f"{tag}_trash", dma=True)
        P.wait("gpsimd", f"{tag}_trash")
        for t in range(NT):
            P.wait("gpsimd", f"{tag}_sl_{t}")
            P.do("gpsimd", lambda e, t=t: e.indirect_dma_start(
                out=tokof_dram[:].unsqueeze(1),
                out_offset=IndirectOffsetOnAxis(slot_i[:, t:t + 1], 0),
                in_=tokidc[:, t:t + 1], in_offset=None),
                ev=f"{tag}_sc_{t}", dma=True)
        P.wait("gpsimd", f"{tag}_sc_{NT-1}")
        P.do("gpsimd", lambda e: e.dma_start(
            out=tk25[:], in_=tokof_dram[0:3200].rearrange("(r p) -> r p", r=25)),
            ev=f"{tag}_tk25", dma=True)
        P.wait("tensor", f"{tag}_tk25")
        P.do("tensor", lambda e: e.transpose(PS[0][:, 0:25], tk25[:], ident[0:25, 0:25]),
             ev=f"{tag}_tkt")
        P.wait("scalar", f"{tag}_tkt")
        P.do("scalar", lambda e: e.copy(out=tokf[:], in_=PS[0][:, 0:25]), ev=f"{tag}_tkf")
        P.wait("vector", f"{tag}_tkf")
        P.do("vector", lambda e: e.tensor_copy(tok_sb[:], tokf[:]), ev=f"{tag}_toksb", )
        P.wait("gpsimd", f"{tag}_toksb")
        P.wait("gpsimd", f"{tag}_xd_{NT-1}")
        for e_i in range(NE):
            for r in range(3):
                P.do("gpsimd", lambda e, e_i=e_i, r=r: e.indirect_dma_start(
                    out=dtm[:, r, :], out_offset=None,
                    in_=xln_dram[:, :],
                    in_offset=IndirectOffsetOnAxis(tok_sb[:, 3 * e_i + r:3 * e_i + r + 1], 0)),
                    ev=f"{tag}_g_{e_i}_{r}", dma=True, dsem=f"g{r}")
            for r in range(3):
                P.wait("tensor", f"{tag}_g_{e_i}_{r}")
                if e_i > 0 and r == 0:
                    P.wait("tensor", f"{tag}_eo_{e_i-1}_2")
                for c in range(8):
                    k = c + r * 8
                    if k >= 2:
                        P.wait("tensor", f"{tag}_df_{e_i}_{r - (1 if c < 2 else 0)}_{(k - 2) % 8}")
                    P.do("tensor", lambda e, r=r, c=c, k=k: e.transpose(
                        PS[k % 2][:, 0:128], dtm[:, r, c * 128:(c + 1) * 128], ident[:]),
                        ev=f"{tag}_dt_{e_i}_{r}_{c}")
                    P.wait("scalar", f"{tag}_dt_{e_i}_{r}_{c}")
                    if e_i > 0 and r == 0 and c == 0:
                        P.wait("scalar", f"{tag}_w1m_{e_i-1}_31")
                    P.do("scalar", lambda e, r=r, c=c, k=k: e.copy(
                        out=dfm[:, c, r * 128:(r + 1) * 128], in_=PS[k % 2][:, 0:128]),
                        ev=f"{tag}_df_{e_i}_{r}_{c}")
            for f in range(32):
                par = (e_i * 32 + f) % 2
                if e_i == 0 and f == 0:
                    P.wait("sync", f"{tag}_tp_{NT-1}_7")
                if e_i * 32 + f >= 2:
                    P.wait("sync", f"{tag}_w1m_{(e_i * 32 + f - 2) // 32}_{(e_i * 32 + f - 2) % 32}")
                P.do("sync", lambda e, e_i=e_i, f=f, par=par: e.dma_start(
                    out=w1buf[:, par, :, :],
                    in_=fW1[e_i, :, f * 128:(f + 1) * 128].rearrange("(c p) n -> p c n", p=128)),
                    ev=f"{tag}_w1l_{e_i}_{f}", dma=True, dsem=f"w1{par}")
            for f in range(32):
                par = (e_i * 32 + f) % 2
                ps = PS[2] if f % 2 == 0 else PS[3]
                P.wait("tensor", f"{tag}_w1l_{e_i}_{f}")
                if f == 0:
                    P.wait("tensor", f"{tag}_df_{e_i}_2_7")
                if f >= 2:
                    P.wait("tensor", f"{tag}_h1_{e_i}_{f-2}")
                for c in range(8):
                    P.do("tensor", lambda e, par=par, c=c, ps=ps: e.matmul(
                        ps[:, 0:384], w1buf[:, par, c, :], dfm[:, c, :],
                        start=(c == 0), stop=(c == 7)),
                        ev=(f"{tag}_w1m_{e_i}_{f}" if c == 7 else None))
                P.wait("scalar", f"{tag}_w1m_{e_i}_{f}")
                P.do("scalar", lambda e, f=f, ps=ps: e.activation(
                    out=h1v(f), in_=ps[:, 0:384], func=ACT.Relu), ev=f"{tag}_h1_{e_i}_{f}")
            for f in range(32):
                par = (e_i * 32 + f) % 2
                if e_i == 0 and f == 0:
                    P.wait("sync", f"{tag}_rps_{NT-1}")
                if e_i * 32 + f >= 2:
                    P.wait("sync", f"{tag}_w2m_{(e_i * 32 + f - 2) // 32}_{(e_i * 32 + f - 2) % 32}")
                P.do("sync", lambda e, e_i=e_i, f=f, par=par: e.dma_start(
                    out=w2buf[:, par, :],
                    in_=fW2[e_i, f * 128:(f + 1) * 128, :]),
                    ev=f"{tag}_w2l_{e_i}_{f}", dma=True, dsem=f"w2{par}")
            for f in range(32):
                par = (e_i * 32 + f) % 2
                P.wait("tensor", f"{tag}_w2l_{e_i}_{f}")
                P.wait("tensor", f"{tag}_h1_{e_i}_{f}")
                for r in range(3):
                    for n in range(2):
                        if r == 0:
                            out_ap = PS[6 + n][:]
                        elif r == 1:
                            out_ap = PS[0 + n][:]
                        else:
                            out_ap = PS[2 + n][:]
                        P.do("tensor", lambda e, f=f, par=par, r=r, n=n, out_ap=out_ap: e.matmul(
                            out_ap, h1v(f)[:, r * 128:(r + 1) * 128],
                            w2buf[:, par, n * 512:(n + 1) * 512],
                            start=(f == 0), stop=(f == 31)),
                            ev=(f"{tag}_w2m_{e_i}_{f}" if (r, n) == (2, 1) else None))
            for r in range(3):
                P.wait("scalar", f"{tag}_w2m_{e_i}_31")
                if r == 0:
                    src0, src1 = PS[6][:], PS[7][:]
                elif r == 1:
                    src0, src1 = PS[0][:], PS[1][:]
                else:
                    src0, src1 = PS[2][:], PS[3][:]
                P.do("scalar", lambda e, src0=src0: e.copy(out=eoutt[:, 0:512], in_=src0))
                P.do("scalar", lambda e, src1=src1: e.copy(out=eoutt[:, 512:1024], in_=src1),
                     ev=f"{tag}_eo_{e_i}_{r}")
                P.wait("gpsimd", f"{tag}_eo_{e_i}_{r}")
                P.do("gpsimd", lambda e, e_i=e_i, r=r: e.indirect_dma_start(
                    out=eout_dram[:, :],
                    out_offset=IndirectOffsetOnAxis(tok_sb[:, 3 * e_i + r:3 * e_i + r + 1], 0),
                    in_=eoutt[:], in_offset=None),
                    ev=f"{tag}_es_{e_i}_{r}", dma=True)
                if r < 2:
                    P.wait("scalar", f"{tag}_es_{e_i}_{r}")
        for t in range(NT):
            P.wait("sync", f"{tag}_es_{NE-1}_2") if t == 0 else None
            if t >= 1:
                P.wait("sync", f"{tag}_bl_{t-1}")
            P.do("sync", lambda e, t=t: e.dma_start(
                out=bl_a[:], in_=eout_dram[t * 128:(t + 1) * 128, :]), dma=True)
            P.do("sync", lambda e, t=t: e.dma_start(
                out=bl_b[:], in_=xln_dram[t * 128:(t + 1) * 128, :]),
                ev=f"{tag}_bld_{t}", dma=True)
            P.wait("vector", f"{tag}_bld_{t}")
            P.do("vector", lambda e, t=t: e.scalar_tensor_tensor(
                xs[:, t, :], bl_a[:], s1b[:, t:t + 1], xs[:, t, :], ALU.mult, ALU.add))
            P.do("vector", lambda e, t=t: e.scalar_tensor_tensor(
                xs[:, t, :], bl_b[:], s0b[:, t:t + 1], xs[:, t, :], ALU.mult, ALU.add),
                ev=f"{tag}_bl_{t}")
        return f"{tag}_bl_{NT-1}"

    CSTOP = int(os.environ.get("CSTOP", "99"))

    def gp_cond_dma(dst_sel, src_ap, ev):
        def fn(h):
            with h.If_eq(hreg, 0):
                h.dma_start(out=dst_sel(0), in_=src_ap).then_inc(
                    P.dsems[("gpsimd", None)], 16)
            with h.Else():
                h.dma_start(out=dst_sel(1), in_=src_ap).then_inc(
                    P.dsems[("gpsimd", None)], 16)
            h.end_ifs()
            return None
        P.dcnt[("gpsimd", None)] += 16
        P.events[ev] = ("d", ("gpsimd", None), P.dcnt[("gpsimd", None)])
        P.streams["gpsimd"].append(("do", fn, None, False, None))

    def emit_attnb(after_ev):
        P.wait("sync", after_ev)
        P.do("sync", lambda e: e.dma_start(
            out=xpark_dram.rearrange("(t p) d -> p t d", p=128), in_=xs[:]),
            ev="c_park", dma=True, dsem="park")
        P.do("sync", lambda e: e.dma_start(
            out=bstg[:], in_=bbq.rearrange("a (c p) -> (a c) p", c=8)), dma=True, dsem="bld")
        P.do("sync", lambda e: e.dma_start(out=halfsb[:], in_=halfsel[:, :]),
             ev="c_bld", dma=True, dsem="bld")
        P.wait("tensor", "c_bld")
        P.do("tensor", lambda e: e.transpose(PS[0][:, 0:8], bstg[:], ident[0:8, 0:8]),
             ev="c_bqt")
        P.wait("scalar", "c_bqt")
        P.do("scalar", lambda e: e.copy(out=bqfm[:], in_=PS[0][:, 0:8]), ev="c_bqf")
        P.wait("sync", "c_bqf")
        P.do("sync", lambda e: e.dma_start(
            out=bstg[:], in_=bbk.rearrange("a (c p) -> (a c) p", c=8)),
            ev="c_bld2", dma=True, dsem="bld")
        P.wait("tensor", "c_bld2")
        P.do("tensor", lambda e: e.transpose(PS[0][:, 0:8], bstg[:], ident[0:8, 0:8]),
             ev="c_bkt")
        P.wait("scalar", "c_bkt")
        P.do("scalar", lambda e: e.copy(out=bkfm[:], in_=PS[0][:, 0:8]), ev="c_bkf")
        P.wait("sync", "c_bkf")
        P.do("sync", lambda e: e.dma_start(out=bstg[0:1, 0:64], in_=bbqi[:, :]),
             ev="c_bld3", dma=True, dsem="bld")
        P.wait("tensor", "c_bld3")
        P.do("tensor", lambda e: e.transpose(PS[0][0:64, 0:1], bstg[0:1, 0:64],
                                             ident[0:1, 0:1]), ev="c_bqit")
        P.wait("scalar", "c_bqit")
        P.do("scalar", lambda e: e.copy(out=qibias[:], in_=PS[0][0:64, 0:1]), ev="c_bqif")
        P.wait("sync", "c_bqif")
        P.do("sync", lambda e: e.dma_start(out=bstg[0:1, 0:64], in_=bbki[:, :]),
             ev="c_bld4", dma=True, dsem="bld")
        P.wait("tensor", "c_bld4")
        P.do("tensor", lambda e: e.transpose(PS[0][0:64, 0:1], bstg[0:1, 0:64],
                                             ident[0:1, 0:1]), ev="c_bkit")
        P.wait("scalar", "c_bkit")
        P.do("scalar", lambda e: e.copy(out=kibias[:], in_=PS[0][0:64, 0:1]), ev="c_bkif")
        P.do("sync", lambda e: e.dma_start(
            out=bv_rep, in_=bbv[:, :].broadcast_to([128, D])), ev="c_bvrep",
            dma=True, dsem="bld")
        for t in range(NT):
            m = t % 2
            P.wait("sync", "c_park") if t == 0 else None
            if t >= 2:
                P.wait("sync", f"c_ln_{t-2}")
            P.do("sync", lambda e, t=t, m=m: e.dma_start(
                out=(bl_a if m == 0 else bl_b),
                in_=xpark_dram[t * 128:(t + 1) * 128, :]),
                ev=f"c_xp_{t}", dma=True, dsem=f"xp{m}")
            xpb = bl_a if m == 0 else bl_b
            P.wait("vector", f"c_xp_{t}")
            if t >= 2:
                P.wait("vector", f"c_tp_{t-2}_7")
            P.do("vector", lambda e, xpb=xpb: e.reduce_sum(mean[:, 0:1], xpb, axis=AX))
            P.do("vector", lambda e: e.tensor_scalar_mul(mean[:, 0:1], mean[:, 0:1], 1.0 / D))
            P.do("vector", lambda e, xpb=xpb: e.tensor_tensor(
                sq_scr[:], xpb, xpb, ALU.mult))
            P.do("vector", lambda e: e.reduce_sum(ssq[:, 0:1], sq_scr[:], axis=AX))
            P.do("vector", lambda e: e.tensor_scalar_mul(rstd[:, 0:1], ssq[:, 0:1], 1.0 / D))
            P.do("vector", lambda e: e.tensor_tensor(ssq[:, 1:2], mean[:, 0:1], mean[:, 0:1], ALU.mult))
            P.do("vector", lambda e: e.tensor_tensor(rstd[:, 0:1], rstd[:, 0:1], ssq[:, 1:2], ALU.subtract))
            P.do("vector", lambda e: e.tensor_scalar_add(rstd[:, 0:1], rstd[:, 0:1], EPS))
            P.do("vector", lambda e: e.reciprocal(rstd[:, 0:1], rstd[:, 0:1]), ev=f"c_var_{t}")
            P.wait("scalar", f"c_var_{t}")
            P.do("scalar", lambda e: e.activation(out=rstd[:, 0:1], in_=rstd[:, 0:1],
                                                  func=ACT.Sqrt), ev=f"c_rstd_{t}")
            P.wait("vector", f"c_rstd_{t}")
            P.do("vector", lambda e, m=m, xpb=xpb: e.tensor_scalar(
                lnt[:, m, :], xpb, mean[:, 0:1], rstd[:, 0:1],
                ALU.subtract, op1=ALU.mult), ev=f"c_ln_{t}")
            P.wait("tensor", f"c_ln_{t}")
            for c in range(8):
                k = t * 8 + c
                if k >= 2:
                    P.wait("tensor", f"c_xff_{(k - 2) // 8}_{(k - 2) % 8}")
                P.do("tensor", lambda e, m=m, c=c, k=k: e.transpose(
                    PS[k % 2][:, 0:128], lnt[:, m, c * 128:(c + 1) * 128], ident[:]),
                    ev=f"c_tp_{t}_{c}")
                P.wait("scalar", f"c_tp_{t}_{c}")
                P.do("scalar", lambda e, c=c, t=t, k=k: e.copy(
                    out=xfull[:, c, t * 128:(t + 1) * 128], in_=PS[k % 2][:, 0:128]),
                    ev=f"c_xff_{t}_{c}")
        if CSTOP < 20:
            return f"c_xff_{NT-1}_7"
        pstage = sq_scr[:]
        for pi, (wsrc, dstd, bfm, pname) in enumerate(
                [(bwq, qfm_dram, bqfm, "q"), (bwk, k_stage, bkfm, "k")]):
            for co in range(8):
                u = co % 2
                gi = pi * 8 + co
                if gi == 0:
                    P.wait("sync", f"c_tp_{NT-1}_7")
                if gi >= 2:
                    P.wait("sync", "c_pm_%d_%d_3" % ((gi - 2) // 8, (gi - 2) % 8))
                P.do("sync", lambda e, wsrc=wsrc, co=co, u=u: e.dma_start(
                    out=wqstr[:, u, :, :],
                    in_=wsrc[:, co * 128:(co + 1) * 128].rearrange("(c p) n -> p c n", p=128)),
                    ev=f"c_wl_{pi}_{co}", dma=True, dsem=f"qw{u}")
                for tg in range(4):
                    sp = tg % 2
                    P.wait("tensor", f"c_wl_{pi}_{co}") if tg == 0 else None
                    if gi == 0 and tg == 0:
                        P.wait("tensor", f"c_xff_{NT-1}_7")
                    if gi * 4 + tg >= 2:
                        P.wait("tensor", "c_pev_%d" % (gi * 4 + tg - 2))
                    for c in range(8):
                        P.do("tensor", lambda e, u=u, c=c, tg=tg, sp=sp: e.matmul(
                            PS[sp][:, 0:512], wqstr[:, u, c, :],
                            xfull[:, c, tg * 512:(tg + 1) * 512],
                            start=(c == 0), stop=(c == 7)),
                            ev=("c_pm_%d_%d_%d" % (pi, co, tg)) if c == 7 else None)
                    P.wait("vector", "c_pm_%d_%d_%d" % (pi, co, tg))
                    if gi * 4 + tg >= 2:
                        P.wait("vector", "c_pd_%d" % (gi * 4 + tg - 2))
                    P.do("vector", lambda e, sp=sp, bfm=bfm, co=co: e.tensor_scalar_add(
                        pstage[:, sp * 512:(sp + 1) * 512], PS[sp][:, 0:512],
                        bfm[:, co:co + 1]),
                        ev="c_pev_%d" % (gi * 4 + tg))
                    P.wait("sync", "c_pev_%d" % (gi * 4 + tg))
                    P.do("sync", lambda e, dstd=dstd, co=co, tg=tg, sp=sp: e.dma_start(
                        out=dstd[co, :, tg * 512:(tg + 1) * 512],
                        in_=pstage[:, sp * 512:(sp + 1) * 512]),
                        ev="c_pd_%d" % (gi * 4 + tg), dma=True, dsem="pst%d" % sp)
        for pi2, (wsrc, bcol, pname) in enumerate([(bwqi, qibias, "qi"), (bwki, kibias, "ki")]):
            if pi2 == 0:
                P.wait("sync", "c_pm_1_6_3")
            else:
                P.wait("sync", "c_im_0_3")
            P.do("sync", lambda e, wsrc=wsrc: e.dma_start(
                out=wqstr[:, 0, :, 0:64],
                in_=wsrc[:, 0:64].rearrange("(c p) n -> p c n", p=128)),
                ev=f"c_wil_{pi2}", dma=True, dsem="qw0")
            for tg in range(4):
                sp = tg % 2
                if tg == 0:
                    P.wait("tensor", f"c_wil_{pi2}")
                    if pi2 == 0:
                        P.wait("tensor", "c_pev_62")
                        P.wait("tensor", "c_pev_63")
                P.wait("tensor", "c_iev_%d" % (pi2 * 4 + tg - 2)) if pi2 * 4 + tg >= 2 else None
                for c in range(8):
                    P.do("tensor", lambda e, c=c, tg=tg, sp=sp: e.matmul(
                        PS[sp][0:64, 0:512], wqstr[:, 0, c, 0:64],
                        xfull[:, c, tg * 512:(tg + 1) * 512],
                        start=(c == 0), stop=(c == 7)),
                        ev=("c_im_%d_%d" % (pi2, tg)) if c == 7 else None)
                P.wait("vector", "c_im_%d_%d" % (pi2, tg))
                if pi2 == 0:
                    P.do("vector", lambda e, tg=tg, bcol=bcol, sp=sp: e.tensor_scalar_add(
                        qidx_fm[0:64, tg * 512:(tg + 1) * 512],
                        PS[sp][0:64, 0:512], bcol[:]),
                        ev="c_iev_%d" % (pi2 * 4 + tg))
                else:
                    P.do("vector", lambda e, tg=tg, bcol=bcol, sp=sp: e.tensor_scalar_add(
                        pstage[0:64, sp * 512:(sp + 1) * 512], PS[sp][0:64, 0:512],
                        bcol[:]),
                        ev="c_iev_%d" % (pi2 * 4 + tg))
                    P.wait("sync", "c_iev_%d" % (pi2 * 4 + tg))
                    P.do("sync", lambda e, tg=tg, sp=sp: e.dma_start(
                        out=ki_stage[:, tg * 512:(tg + 1) * 512],
                        in_=pstage[0:64, sp * 512:(sp + 1) * 512]),
                        ev="c_kid_%d" % tg, dma=True, dsem="pst%d" % sp)
        P.do("sync", lambda e: e.dma_start(
            out=vws[:], in_=bwv.rearrange("(c p) n -> p c n", p=128)),
            ev="c_vwl", dma=True, dsem="qw1")
        for t in range(NT):
            P.wait("tensor", "c_vwl") if t == 0 else None
            if t >= 1:
                P.wait("tensor", "c_vbd_%d" % (t - 1))
            for c in range(8):
                for n in range(2):
                    P.do("tensor", lambda e, c=c, n=n, t=t: e.matmul(
                        PS[6 + n][:], xfull[:, c, t * 128:(t + 1) * 128],
                        vws[:, c, n * 512:(n + 1) * 512],
                        start=(c == 0), stop=(c == 7)),
                        ev=("c_vm_%d" % t) if (c == 7 and n == 1) else None)
            P.wait("vector", "c_vm_%d" % t)
            P.do("vector", lambda e: e.tensor_tensor(
                vstg[:, 0:512], PS[6][:], bv_rep[:, 0:512], ALU.add))
            P.do("vector", lambda e: e.tensor_tensor(
                vstg[:, 512:1024], PS[7][:], bv_rep[:, 512:1024], ALU.add),
                ev="c_vb_%d" % t)
            P.wait("sync", "c_vb_%d" % t)
            P.do("sync", lambda e, t=t: e.dma_start(
                out=v_stage[t * 128:(t + 1) * 128, :], in_=vstg),
                ev="c_vbd_%d" % t, dma=True, dsem="vst")
        if CSTOP < 30:
            return "c_vbd_%d" % (NT - 1)
        P.wait("gpsimd", "c_bld")
        P.do("gpsimd", lambda e: e.dma_start(out=bar_in[:, :], in_=zeros4_h.ap()[0:1, 0:8]),
             ev="c_barz", dma=True)
        P.do("gpsimd", lambda e: e.reg_load(hreg, halfsb[:1, :1]), ev="c_hreg")
        P.wait("gpsimd", "c_pd_%d" % (1 * 8 * 4 + 7 * 4 + 3))
        P.wait("gpsimd", "c_kid_3")
        P.wait("gpsimd", "c_vbd_%d" % (NT - 1))
        gp_cond_dma(lambda hf: sh_k[hf, :, :, :], k_stage[:, :, :], "c_shk")
        gp_cond_dma(lambda hf: sh_v[hf, :, :], v_stage[:, :], "c_shv")
        gp_cond_dma(lambda hf: sh_ki[hf, :, :], ki_stage[:, :], "c_shki")
        P.wait("gpsimd", "c_shki")
        P.wait("gpsimd", "c_barz")
        P.do("gpsimd", lambda e: e.collective_compute(
            "AllReduce", ALU.add, replica_groups=ALL8,
            ins=[bar_in[:, :].opt()], outs=[bar_out[:, :].opt()]), ev="c_bar")
        if CSTOP < 32:
            return "c_bar"
        P.wait("sync", "c_bar")
        P.do("sync", lambda e: e.dma_start(
            out=kidx_all[:, 0:2048], in_=sh_ki[0, :, :]), dma=True, dsem="kia")
        P.do("sync", lambda e: e.dma_start(
            out=kidx_all[:, 2048:4096], in_=sh_ki[1, :, :]), ev="c_kia", dma=True, dsem="kia")
        if CSTOP < 34:
            return "c_kia"
        for st in range(NT):
            for ac in range(7):
                sp = ac % 2
                gi = st * 7 + ac
                if gi == 0:
                    P.wait("tensor", "c_kia")
                    P.wait("tensor", "c_iev_6")
                    P.wait("tensor", "c_iev_7")
                if gi >= 2:
                    P.wait("tensor", "c_iex_%d" % (gi - 2))
                if ac == 0 and st >= 1:
                    P.wait("tensor", "c_thr_%d" % (st - 1))
                P.do("tensor", lambda e, st=st, ac=ac, sp=sp: e.matmul(
                    PS[sp][:, 0:512], qidx_fm[0:64, st * 128:(st + 1) * 128],
                    kidx_all[0:64, ac * 512:(ac + 1) * 512],
                    start=True, stop=True),
                    ev="c_ips_%d" % gi)
                P.wait("scalar", "c_ips_%d" % gi)
                P.do("scalar", lambda e, ac=ac, sp=sp: e.copy(
                    out=Ibuf[:, ac * 512:(ac + 1) * 512], in_=PS[sp][:, 0:512]),
                    ev="c_iex_%d" % gi)
            P.wait("vector", "c_iex_%d" % (st * 7 + 6))
            for rnd in range(8):
                P.do("vector", lambda e: e.max(out=t8b[:], in_=Ibuf))
                P.do("vector", lambda e: e.match_replace(
                    out=Ibuf, in_to_replace=t8b[:], in_values=Ibuf, imm_value=-1e30))
            P.do("vector", lambda e, st=st: e.tensor_copy(thrb[:, st:st + 1], t8b[:, 7:8]),
                 ev="c_thr_%d" % st)
        if CSTOP < 36:
            return "c_thr_%d" % (NT - 1)
        P.wait("tensor", "c_thr_%d" % (NT - 1))
        P.do("tensor", lambda e: e.transpose(PS[0][0:16, 0:128], thrb[:], ident[:]),
             ev="c_tht")
        P.wait("scalar", "c_tht")
        P.do("scalar", lambda e: e.copy(out=tk25[0:16, :], in_=PS[0][0:16, 0:128]),
             ev="c_thev")
        P.wait("gpsimd", "c_thev")
        P.do("gpsimd", lambda e: e.dma_start(
            out=cc_a_in[0:2048].rearrange("(a b) -> a b", a=16), in_=tk25[0:16, :]),
            ev="c_thd", dma=True)
        P.wait("gpsimd", "c_thd")
        P.do("gpsimd", lambda e: e.dma_start(
            out=thr_rep, in_=cc_a_in[0:2048].unsqueeze(0).broadcast_to([128, 2048])),
            ev="c_threp", dma=True)
        if CSTOP < 40:
            return "c_threp"
        P.wait("vector", "c_threp")
        CMASKN = int(os.environ.get("CMASKN", "28"))
        for a in range(CMASKN):
            if a == 0:
                P.wait("tensor", "c_thev")
            for sc in range(4):
                P.do("tensor", lambda e, a=a, sc=sc: e.matmul(
                    PS[sc][:, 0:512], kidx_all[0:64, a * 128:(a + 1) * 128],
                    qidx_fm[0:64, sc * 512:(sc + 1) * 512], start=True, stop=True),
                    ev=("c_mps_%d" % a) if sc == 3 else None)
            P.wait("vector", "c_mps_%d" % a)
            CMASK = int(os.environ.get("CMASK", "2"))
            for sc in range(4):
                if CMASK >= 2:
                    P.do("vector", lambda e, a=a, sc=sc: e.tensor_tensor(
                        mask_u8[:, a, sc * 512:(sc + 1) * 512], PS[sc][:, 0:512],
                        thr_rep[:, sc * 512:(sc + 1) * 512], ALU.is_ge),
                        ev=("c_msk_%d" % a) if sc == 3 else None)
                else:
                    P.do("vector", lambda e, a=a, sc=sc: e.tensor_copy(
                        Wbuf[:, sc * 512:(sc + 1) * 512], PS[sc][:, 0:512]),
                        ev=("c_msk_%d" % a) if sc == 3 else None)
            if a < 27:
                P.wait("tensor", "c_msk_%d" % a)
        if CSTOP < 50:
            return "c_msk_%d" % (CMASKN - 1)
        P.wait("vector", "c_bvrep")
        P.do("vector", lambda e: e.memset(vstage[:, :, 64:65], 1.0), ev="c_vone")
        for h in range(16):
            u = h % 2
            if h < 2:
                P.wait("sync", "c_vm_%d" % (NT - 1))
            if h >= 2:
                P.wait("sync", "c_S_%d_31" % (h - 2))
            P.do("sync", lambda e, h=h, u=u: e.dma_start(
                out=kbuf[:, u, 0:2048], in_=sh_k[0, h // 2, (h % 2) * 64:(h % 2) * 64 + 64, :]),
                dma=True, dsem="kq%d" % u)
            P.do("sync", lambda e, h=h, u=u: e.dma_start(
                out=kbuf[:, u, 2048:4096], in_=sh_k[1, h // 2, (h % 2) * 64:(h % 2) * 64 + 64, :]),
                dma=True, dsem="kq%d" % u)
            P.do("sync", lambda e, h=h, u=u: e.dma_start(
                out=qbuf[:, u, :], in_=qfm_dram[h // 2, (h % 2) * 64:(h % 2) * 64 + 64, :]),
                ev="c_kq_%d" % h, dma=True, dsem="kq%d" % u)
            if h >= 1:
                P.wait("sync", "c_o_%d_31" % (h - 1))
            P.wait("sync", "c_vone") if h == 0 else None
            for hf in range(2):
                P.do("sync", lambda e, h=h, hf=hf: e.dma_start(
                    out=vstage[:, hf * 16:(hf + 1) * 16, 0:64],
                    in_=sh_v[hf, :, h * 64:(h + 1) * 64].rearrange("(j p) e -> p j e", p=128)),
                    ev=("c_vs_%d" % h) if hf == 1 else None, dma=True, dsem="vsl")
            for a in range(32):
                P.wait("tensor", "c_kq_%d" % h) if a == 0 else None
                if h == 0 and a == 0:
                    P.wait("tensor", "c_msk_%d" % (CMASKN - 1))
                if h >= 1 and a == 0:
                    P.wait("tensor", "c_ofev_%d" % (h - 1))
                if a >= 1:
                    P.wait("tensor", "c_exp_%d_%d" % (h, a - 1))
                elif h >= 1:
                    P.wait("tensor", "c_exp_%d_31" % (h - 1))
                for sc in range(4):
                    P.do("tensor", lambda e, h=h, u=u, a=a, sc=sc: e.matmul(
                        PS[sc][:, 0:512], kbuf[:, u, a * 128:(a + 1) * 128],
                        qbuf[:, u, sc * 512:(sc + 1) * 512], start=True, stop=True),
                        ev=("c_S_%d_%d" % (h, a)) if sc == 3 else None)
                if a >= 1:
                    P.wait("tensor", "c_W_%d_%d" % (h, a - 1))
                    if a == 1:
                        P.wait("tensor", "c_vs_%d" % h)
                    for sc in range(4):
                        P.do("tensor", lambda e, h=h, a=a, sc=sc: e.matmul(
                            PS[4 + sc][0:65, 0:512], vstage[:, a - 1, :],
                            Wbuf[:, sc * 512:(sc + 1) * 512],
                            start=(a - 1 == 0), stop=False),
                            ev=("c_o_%d_%d" % (h, a - 1)) if sc == 3 else None)
                P.wait("scalar", "c_S_%d_%d" % (h, a))
                if a >= 1:
                    P.wait("scalar", "c_o_%d_%d" % (h, a - 1))
                for sc in range(4):
                    P.do("scalar", lambda e, h=h, a=a, sc=sc: e.activation(
                        out=Wbuf[:, sc * 512:(sc + 1) * 512], in_=PS[sc][:, 0:512],
                        func=ACT.Exp, bias=0.0, scale=SCALE_B),
                        ev=("c_exp_%d_%d" % (h, a)) if sc == 3 else None)
                if a < 28:
                    P.wait("vector", "c_exp_%d_%d" % (h, a))
                    P.do("vector", lambda e, h=h, a=a: e.tensor_tensor(
                        Wbuf[:, :], Wbuf[:, :], mask_u8[:, a, :], ALU.mult),
                        ev="c_W_%d_%d" % (h, a))
                else:
                    P.events["c_W_%d_%d" % (h, a)] = P.events["c_exp_%d_%d" % (h, a)]
            P.wait("tensor", "c_W_%d_31" % h)
            for sc in range(4):
                P.do("tensor", lambda e, h=h, sc=sc: e.matmul(
                    PS[4 + sc][0:65, 0:512], vstage[:, 31, :],
                    Wbuf[:, sc * 512:(sc + 1) * 512], start=False, stop=True),
                    ev=("c_o_%d_31" % h) if sc == 3 else None)
            P.wait("scalar", "c_o_%d_31" % h)
            if h >= 1:
                P.wait("scalar", "c_ofd_%d" % (h - 1))
            for sc in range(4):
                P.do("scalar", lambda e, sc=sc: e.copy(
                    out=ofm_s[:, sc * 512:(sc + 1) * 512], in_=PS[4 + sc][0:65, 0:512]),
                    ev=("c_ofev_%d" % h) if sc == 3 else None)
            P.wait("sync", "c_ofev_%d" % h)
            P.do("sync", lambda e, h=h: e.dma_start(
                out=ofm_dram[h, :, :], in_=ofm_s[0:65, :]),
                ev="c_ofd_%d" % h, dma=True, dsem="ofm")
        if CSTOP < 60:
            return "c_ofd_15"
        dstg = wv[:].rearrange("p a b -> p (a b)")[0:64, 0:2048]
        P.do("sync", lambda e: e.dma_start(
            out=bo_rep2[:, 0:1024], in_=bbo[:, :].broadcast_to([128, D])),
            ev="c_borep", dma=True, dsem="bld")
        for half in range(2):
            P.wait("sync", "c_S_15_31")
            if half == 1:
                P.wait("sync", "c_wo_done_0")
            P.do("sync", lambda e, half=half: e.dma_start(
                out=wo_s64[:, :, :],
                in_=bwo[:, half * 512:(half + 1) * 512].rearrange("(h p) n -> p h n", p=64)),
                ev="c_wol_%d" % half, dma=True, dsem="wol")
            for sc in range(NT):
                par = sc % 2
                gi = half * NT + sc
                if gi == 0:
                    P.wait("sync", "c_ofd_15")
                if gi >= 2:
                    P.wait("sync", "c_wom_%d_%d" % ((gi - 2) // NT, (gi - 2) % NT))
                for hh2 in range(16):
                    P.do("sync", lambda e, hh2=hh2, sc=sc, par=par: e.dma_start(
                        out=ostage[:, par, hh2, :],
                        in_=ofm_dram[hh2, 0:64, sc * 128:(sc + 1) * 128]),
                        ev=("c_ost_%d_%d" % (half, sc)) if hh2 == 15 else None,
                        dma=True, dsem="ost%d" % par)
                P.do("sync", lambda e, sc=sc, par=par: e.dma_start(
                    out=(bl_a if par == 0 else bl_b),
                    in_=xpark_dram[sc * 128:(sc + 1) * 128, :]),
                    ev="c_xpr_%d_%d" % (half, sc), dma=True, dsem="xp%d" % par)
                P.wait("gpsimd", "c_ofd_15") if gi == 0 else None
                if gi >= 1:
                    P.wait("gpsimd", "c_dsc_%d_%d" % ((gi - 1) // NT, (gi - 1) % NT))
                P.do("gpsimd", lambda e, sc=sc: e.dma_start(
                    out=cc_a_in[0:2048].rearrange("(a b) -> a b", a=16),
                    in_=ofm_dram[:, 64, sc * 128:(sc + 1) * 128]),
                    ev="c_dd1_%d_%d" % (half, sc), dma=True, dsem="dd")
                P.wait("gpsimd", "c_dd1_%d_%d" % (half, sc))
                P.do("gpsimd", lambda e: e.dma_start(
                    out=dstg, in_=cc_a_in[0:2048].unsqueeze(0).broadcast_to([64, 2048])),
                    ev="c_dd2_%d_%d" % (half, sc), dma=True, dsem="dd")
                P.wait("vector", "c_dd2_%d_%d" % (half, sc))
                P.wait("vector", "c_ost_%d_%d" % (half, sc))
                P.do("vector", lambda e: e.reciprocal(dstg, dstg))
                P.do("vector", lambda e, par=par: e.tensor_tensor(
                    ostage[:, par, :, :], ostage[:, par, :, :],
                    dstg.rearrange("p (a b) -> p a b", a=16), ALU.mult),
                    ev="c_dsc_%d_%d" % (half, sc))
                P.wait("tensor", "c_wol_%d" % half) if sc == 0 else None
                P.wait("tensor", "c_dsc_%d_%d" % (half, sc))
                if gi >= 1:
                    P.wait("tensor", "c_x3_%d_%d" % ((gi - 1) // NT, (gi - 1) % NT))
                for hh2 in range(16):
                    P.do("tensor", lambda e, hh2=hh2, par=par: e.matmul(
                        PS[4][:, 0:512], ostage[:, par, hh2, :], wo_s64[:, hh2, :],
                        start=(hh2 == 0), stop=(hh2 == 15)),
                        ev=("c_wom_%d_%d" % (half, sc)) if hh2 == 15 else None)
                P.wait("vector", "c_wom_%d_%d" % (half, sc))
                P.wait("vector", "c_borep") if gi == 0 else None
                P.wait("vector", "c_xpr_%d_%d" % (half, sc))
                xpb = bl_a if par == 0 else bl_b
                P.do("vector", lambda e, half=half: e.tensor_tensor(
                    Wbuf[:, 0:512], PS[4][:, 0:512],
                    bo_rep2[:, half * 512:(half + 1) * 512], ALU.add))
                P.do("vector", lambda e, sc=sc, half=half, xpb=xpb: e.tensor_tensor(
                    xs[:, sc, half * 512:(half + 1) * 512], Wbuf[:, 0:512],
                    xpb[:, half * 512:(half + 1) * 512], ALU.add),
                    ev="c_x3_%d_%d" % (half, sc))
            P.events["c_wo_done_%d" % half] = P.events["c_x3_%d_%d" % (half, NT - 1)]
        return "c_x3_1_%d" % (NT - 1)


    if phases >= 2:
        last_ev_name = emit_ffn("f1", last_ev)
    else:
        last_ev_name = last_ev
    if phases >= 3:
        last_ev_name = emit_attnb(last_ev_name)
    if phases >= 4:
        last_ev_name = emit_ffn("f2", last_ev_name)

    P.wait("sync", last_ev_name)
    if KDBG2 and phases >= 3:
        P.do("sync", lambda e: e.dma_start(out=d_shk[0], in_=sh_k[0, 0, :, 0:256]), dma=True)
        P.do("sync", lambda e: e.dma_start(out=d_shk[1], in_=sh_k[1, 0, :, 0:256]), dma=True)
        P.do("sync", lambda e: e.dma_start(out=d_qfm[:, :], in_=qfm_dram[0, :, 0:256]), dma=True)
        P.do("sync", lambda e: e.dma_start(out=d_ofm[0], in_=ofm_dram[0, :, 0:256]), dma=True)
        P.do("sync", lambda e: e.dma_start(out=d_ofm[1], in_=ofm_dram[8, :, 0:256]), dma=True)
        P.do("sync", lambda e: e.dma_start(out=d_thr[:, :], in_=cc_a_in[0:2048].rearrange("(a b) -> a b", a=16)), dma=True)
        P.do("sync", lambda e: e.dma_start(out=d_shv[0], in_=sh_v[0, 0:128, 0:64]), dma=True)
        P.do("sync", lambda e: e.dma_start(out=d_shv[1], in_=sh_v[1, 0:128, 0:64]), ev="dbg2_done", dma=True)
        P.wait("sync", "dbg2_done")
    P.do("sync", lambda e: e.dma_start(
        out=y_out.rearrange("(t p) d -> p t d", p=128), in_=xs[:]), ev="y_done", dma=True)
    if KDBG:
        P.do("sync", lambda e: e.dma_start(out=dbg1[:, 0:16], in_=mean[:]), dma=True)
        P.do("sync", lambda e: e.dma_start(out=dbg1[:, 16:32], in_=rstd[:]), dma=True)
        P.do("sync", lambda e: e.dma_start(out=dbg1[:, 32:48], in_=ssq[:]), dma=True)
        P.do("sync", lambda e: e.dma_start(out=dbg1[:, 48:80], in_=qv_dbg[:]), dma=True)
        P.do("sync", lambda e: e.dma_start(out=dbg2[:, :], in_=q_sm[:]), dma=True)
        P.do("sync", lambda e: e.dma_start(out=dbg3[0:64, :], in_=ctx2[0:64, :]), dma=True)
        P.do("sync", lambda e: e.dma_start(out=dbg4[:, :], in_=rsc_rep), ev="dbg_done", dma=True)
        P.wait("sync", "dbg_done")
    P.wait("sync", "y_done")

    block = es.enter_context(nc.Block())
    P.realize(block)
    es.close()
    return nc


def kernel(L0_attn_a_x, L0_attn_a_Wq, L0_attn_a_Wk, L0_attn_a_Wv, L0_attn_a_Wo,
           L0_attn_a_bo, L0_attn_b_wq_w, L0_attn_b_wq_b, L0_attn_b_wk_w,
           L0_attn_b_wk_b, L0_attn_b_wv_w, L0_attn_b_wv_b, L0_attn_b_wo_w,
           L0_attn_b_wo_b, L0_attn_b_wqi_w, L0_attn_b_wqi_b, L0_attn_b_wki_w,
           L0_attn_b_wki_b, L0_ffn_Wr, L0_ffn_W1, L0_ffn_W2):
    nc = build_nc(PHASES)
    x = np.ascontiguousarray(np.asarray(L0_attn_a_x, dtype=np.float32))
    B, S, _ = x.shape
    in_maps = []
    for c in range(8):
        b, h = c // 2, c % 2
        m = {
            "x": np.ascontiguousarray(x[b, h * T:(h + 1) * T, :]),
            "aWq": np.asarray(L0_attn_a_Wq, np.float32),
            "aWk": np.asarray(L0_attn_a_Wk, np.float32),
            "aWv": np.asarray(L0_attn_a_Wv, np.float32),
            "aWo": np.asarray(L0_attn_a_Wo, np.float32),
            "abo": np.asarray(L0_attn_a_bo, np.float32).reshape(1, D),
        }
        if PHASES >= 2:
            m.update({
                "fWr": np.asarray(L0_ffn_Wr, np.float32),
                "fW1": np.asarray(L0_ffn_W1, np.float32),
                "fW2": np.asarray(L0_ffn_W2, np.float32),
                "maskprev": (np.arange(8) < c).astype(np.float32).reshape(8, 1),
                "onehotc": (np.arange(8) == c).astype(np.float32).reshape(8, 1),
            })
        if PHASES >= 3:
            m.update({
                "bwq": np.asarray(L0_attn_b_wq_w, np.float32),
                "bwk": np.asarray(L0_attn_b_wk_w, np.float32),
                "bwv": np.asarray(L0_attn_b_wv_w, np.float32),
                "bwo": np.asarray(L0_attn_b_wo_w, np.float32),
                "bbq": np.asarray(L0_attn_b_wq_b, np.float32).reshape(1, D),
                "bbk": np.asarray(L0_attn_b_wk_b, np.float32).reshape(1, D),
                "bbv": np.asarray(L0_attn_b_wv_b, np.float32).reshape(1, D),
                "bbo": np.asarray(L0_attn_b_wo_b, np.float32).reshape(1, D),
                "bwqi": np.asarray(L0_attn_b_wqi_w, np.float32),
                "bwki": np.asarray(L0_attn_b_wki_w, np.float32),
                "bbqi": np.asarray(L0_attn_b_wqi_b, np.float32).reshape(1, 64),
                "bbki": np.asarray(L0_attn_b_wki_b, np.float32).reshape(1, 64),
                "halfsel": np.array([[h]], dtype=np.int32),
            })
        in_maps.append(m)
    res = run_bass_kernel_spmd(nc, in_maps, core_ids=list(range(8)))
    out = np.empty((B, S, D), dtype=np.float32)
    for c in range(8):
        b, h = c // 2, c % 2
        out[b, h * T:(h + 1) * T, :] = res.results[c]["y"]
    return out
